# version 08
# baseline (speedup 1.0000x reference)
import numpy as np
import ml_dtypes
import concourse.bass as bass
import concourse.mybir as mybir
from concourse.bass_utils import run_bass_kernel_spmd

F32 = mybir.dt.float32
BF16 = mybir.dt.bfloat16
I32 = mybir.dt.int32
AF = mybir.ActivationFunctionType
ALU = mybir.AluOpType
AX = mybir.AxisListType

S = 4096
H = 2048
FF = 8192
NH = 16
NST = S // 128
NHT = H // 128
NIT = FF // 128
IBLK = 256
NIB = FF // IBLK

INV_SQRT_DH = float(1.0 / np.sqrt(128.0))
EPS = 1e-5
ENGS = ("tensor", "vector", "scalar", "gpsimd", "sync")
SEM_ROLL = 50000


class Buf:
    __slots__ = ("name", "lo", "hi", "writers", "readers", "aliases", "dsem")

    def __init__(self, name, lo=-1, hi=-1):
        self.name = name
        self.lo, self.hi = lo, hi
        self.writers = {}
        self.readers = {}
        self.aliases = [self]
        self.dsem = None


CHANNELS = ("tensor", "vector", "scalar", "gpsimd", "sync", "scalar_dma")


class Prog:

    def __init__(self, nc, stack):
        self.nc = nc
        self.stack = stack
        self.sched = []
        self.cnt = {e: 0 for e in CHANNELS}
        self.sems = {e: [stack.enter_context(nc.semaphore(f"ch_{e}0"))] for e in CHANNELS}
        self.last_wait = {e: {} for e in CHANNELS}
        self.sbuf_bufs = []
        self.dma_sems = []
        self.cur_scope = "init"

    def buf(self, name, lo=-1, hi=-1):
        b = Buf(name, lo, hi)
        if lo >= 0:
            for o in self.sbuf_bufs:
                if o.lo < hi and lo < o.hi:
                    o.aliases.append(b)
                    b.aliases.append(o)
            self.sbuf_bufs.append(b)
        return b

    def _buf_dsem(self, b, chan):
        if b.dsem is None:
            b.dsem = {}
        if chan not in b.dsem:
            sem = self.stack.enter_context(self.nc.semaphore(f"ds_{b.name}_{chan}"))
            b.dsem[chan] = [sem, 0]
            self.dma_sems.append(b.dsem[chan])
        return b.dsem[chan]

    def emit(self, eng, fn, reads=(), writes=(), note="", mm_w=None):
        note = self.cur_scope
        chan = eng
        real_eng = "scalar" if eng == "scalar_dma" else eng
        dma = chan in ("sync", "gpsimd", "scalar_dma")
        if dma:
            tgt = None
            for b in writes:
                if b.lo >= 0:
                    tgt = b
                    break
            if tgt is None:
                for b in reads:
                    if b.lo >= 0:
                        tgt = b
                        break
            if tgt is None:
                tgt = writes[0] if writes else reads[0]
            dsem = self._buf_dsem(tgt, chan)
            sem = dsem[0]
            inc = 16
        else:
            inc = 1
            if self.cnt[chan] + inc > SEM_ROLL:
                self.sems[chan].append(self.stack.enter_context(
                    self.nc.semaphore(f"ch_{chan}{len(self.sems[chan])}")))
                self.cnt[chan] = 0
            sem = self.sems[chan][-1]
        waits = {}

        def need(ev, rel):
            if ev is None:
                return
            esem, val, echan = ev
            if echan == chan:
                if chan == "tensor":
                    return
            key = id(esem)
            if self.last_wait[chan].get(key, -1) >= val:
                return
            if key in waits:
                if val > waits[key][1]:
                    waits[key] = (esem, val)
            else:
                waits[key] = (esem, val)

        for b in reads:
            for a in b.aliases:
                for ev in a.writers.values():
                    need(ev, "raw")
        for b in writes:
            for a in b.aliases:
                for ev in a.writers.values():
                    need(ev, "waw")
                for ev in a.readers.values():
                    need(ev, "war")
        for key, (esem, val) in waits.items():
            self.last_wait[chan][key] = max(
                self.last_wait[chan].get(key, -1), val)
        if dma:
            dsem[1] += inc
            ev = (sem, dsem[1], chan)
        else:
            self.cnt[chan] += inc
            ev = (sem, self.cnt[chan], chan)
        if mm_w is not None:
            mm_w = (note,) + tuple(mm_w)
        self.sched.append((real_eng, fn, list(waits.values()), sem, inc, note, mm_w))
        for b in writes:
            old = b.writers.get(id(sem))
            if old is None or ev[1] > old[1]:
                b.writers[id(sem)] = ev
            b.readers = {}
        for b in reads:
            old = b.readers.get(id(sem))
            if old is None or ev[1] > old[1]:
                b.readers[id(sem)] = ev
        return ev

    def finalize(self, block):
        per_eng = {e: [] for e in ENGS}
        for eng, fn, waits, sem, inc, note, mm_w in self.sched:
            per_eng[eng].append((fn, waits, sem, inc, note, mm_w))
        totals = []
        for e in CHANNELS:
            if self.cnt[e] > 0:
                totals.append((self.sems[e][-1], self.cnt[e]))
        for sem, cnt in self.dma_sems:
            if cnt > 0:
                totals.append((sem, cnt))

        nc = self.nc

        def make(engine_name):
            def body(eng):
                cur = [None, None]
                last_w = [None]
                for fn, waits, sem, inc, note, mm_w in per_eng[engine_name]:
                    if note != cur[0]:
                        if cur[0] is not None:
                            nc.leave_named_scope(cur[0], cur[1], False)
                        sid, _ = nc.enter_named_scope(note, False)
                        cur[0], cur[1] = note, sid
                    for wsem, wval in waits:
                        eng.wait_ge(wsem, wval)
                    inst = fn(eng)
                    inst.then_inc(sem, inc)
                    if engine_name == "tensor":
                        if mm_w is not None and mm_w == last_w[0]:
                            self.skip_ldw_names.add(inst.ins.name)
                        last_w[0] = mm_w
                if cur[0] is not None:
                    nc.leave_named_scope(cur[0], cur[1], False)
                if engine_name == "sync":
                    for tsem, tval in totals:
                        eng.wait_ge(tsem, tval)
            return body

        self.skip_ldw_names = set()
        block.sync(make("sync"))
        block.scalar(make("scalar"))
        block.vector(make("vector"))
        block.tensor(make("tensor"))
        block.gpsimd(make("gpsimd"))
        if self.skip_ldw_names:
            for func in nc.m.functions:
                for bb in func.blocks:
                    for idx, ins in enumerate(bb.instructions):
                        if ins.name in self.skip_ldw_names:
                            ins2 = ins.__replace__(ldweights=False)
                            bb.instructions[idx] = ins2
                            nc.register_instruction(ins2, overwrite=True)
        n = {e: len(per_eng[e]) for e in ENGS}
        return n


def build(debug=False, upto=9):
    nc = bass.Bass("TRN2", target_bir_lowering=False)

    x_h = nc.dram_tensor("x", [S, H], F32, kind="ExternalInput").ap()
    wqkv_h = nc.dram_tensor("wqkv", [H, 3 * H], F32, kind="ExternalInput").ap()
    wo_h = nc.dram_tensor("wo", [H, H], F32, kind="ExternalInput").ap()
    wgT_h = nc.dram_tensor("wgT", [H, FF], I32, kind="ExternalInput").ap()
    wuT_h = nc.dram_tensor("wuT", [H, FF], I32, kind="ExternalInput").ap()
    wdT_h = nc.dram_tensor("wdT", [FF, H], I32, kind="ExternalInput").ap()
    gsc_h = nc.dram_tensor("gsc", [1, FF], F32, kind="ExternalInput").ap()
    usc_h = nc.dram_tensor("usc", [1, FF], F32, kind="ExternalInput").ap()
    dsc_h = nc.dram_tensor("dsc", [1, H], F32, kind="ExternalInput").ap()
    out_h = nc.dram_tensor("out", [S, H], F32, kind="ExternalOutput").ap()

    dk = "ExternalOutput" if debug else "Internal"
    xln_h = nc.dram_tensor("xln_s", [S, H], BF16, kind=dk).ap()
    x1_h = nc.dram_tensor("x1_s", [S, H], F32, kind=dk).ap()
    x2_h = nc.dram_tensor("x2_s", [S, H], F32, kind=dk).ap()
    hact_h = nc.dram_tensor("hact_s", [NIT, 128, S], BF16, kind=dk).ap()
    vals_h = nc.dram_tensor("vals_s", [NH, 128, S], BF16, kind=dk).ap()
    wgbf_h = nc.dram_tensor("wgbf_s", [NIB, 128, NHT * IBLK], BF16, kind="Internal").ap()
    wubf_h = nc.dram_tensor("wubf_s", [NIB, 128, NHT * IBLK], BF16, kind="Internal").ap()
    wdbf_h = nc.dram_tensor("wdbf_s", [2, 128, NIT * 1024], BF16, kind="Internal").ap()
    dstat_h = nc.dram_tensor("dstat_s", [NST, 128, 8], F32, kind=dk).ap() if debug else None

    from contextlib import ExitStack
    with ExitStack() as stack:
        ent = stack.enter_context
        A = nc.alloc_sbuf_tensor_at

        c_const = A("c_const", [128, 8], F32, offset=20480)
        c_ones = A("c_ones", [128, 128], BF16, offset=20512)
        c_bn = A("c_bn", [128, 48], F32, offset=20768)
        c_sm = A("c_sm", [128, 16], F32, offset=20960)
        c_mv = [c_sm[:, 0:2], c_sm[:, 8:10]]
        c_rs = [c_sm[:, 2:3], c_sm[:, 10:11]]
        c_nmrs = [c_sm[:, 3:4], c_sm[:, 11:12]]
        c_zp = A("c_zp", [128, 8], F32, offset=21024)
        c_zr = A("c_zr", [128, 8], F32, offset=21056)
        c_z = [c_zr[:, 0:1], c_zr[:, 1:2]]
        c_r = [c_zr[:, 4:5], c_zr[:, 5:6]]
        ST = 23040
        st_x = [A(f"st_x{i}", [128, 2048], F32, offset=ST + i * 8192) for i in range(2)]
        st_q = [A(f"st_q{i}", [128, 1024], F32, offset=ST + i * 4096) for i in range(4)]
        st_tok = [A("st_tok0", [128, 2048], BF16, offset=43520 + 40960),
                  A("st_tok1", [128, 2048], BF16, offset=43520 + 47104)]
        WK = 43520
        w_g = [A(f"w_g{i}", [128, NHT, IBLK], BF16, offset=WK + i * 8192) for i in range(2)]
        w_u = [A(f"w_u{i}", [128, NHT, IBLK], BF16, offset=WK + 16384 + i * 8192) for i in range(2)]
        sg_t = [A(f"sg{i}", [128, 512], BF16, offset=WK + 32768 + i * 1024) for i in range(4)]
        su_t = [A(f"su{i}", [128, 512], BF16, offset=WK + 36864 + i * 1024) for i in range(4)]
        hh_t = [A(f"hh{i}", [128, 512], BF16, offset=WK + 40960 + i * 1024) for i in range(4)]
        c_gbc = A("c_gbc", [128, IBLK], F32, offset=WK + 45056)
        c_ubc = A("c_ubc", [128, IBLK], F32, offset=WK + 45056 + IBLK * 4)
        hring = [A(f"hring{i}", [128, NIT, 128], BF16, offset=WK + i * 16384) for i in range(3)]
        wdst = [A(f"wdst{i}", [128, 512], I32, offset=WK + 47104 + i * 2048) for i in range(2)]
        wdso = [A(f"wdso{i}", [128, 512], BF16, offset=WK + 51200 + i * 1024) for i in range(2)]
        dbc2 = A("dbc2", [128, 512], F32, offset=21088)
        kh = [A("kh0", [128, S], BF16, offset=WK),
              A("kh1", [128, S], BF16, offset=WK + 8192)]
        vh = A("vh", [128, S], BF16, offset=WK + 16384)
        qring = A("qring", [128, 1024], BF16, offset=WK + 24576)
        e_blk = [A("e_blk0", [128, S], BF16, offset=WK + 26624),
                 A("e_blk1", [128, S], BF16, offset=ST)]
        cacc_lo = A("cacc_lo", [128, 2048], F32, offset=WK + 34816)
        cacc_hi = A("cacc_hi", [128, 2048], F32, offset=ST + 12288)
        wk_s = A("wk_s", [128, NHT, 128], BF16, offset=WK + 43008)
        wv_s = A("wv_s", [128, NHT, 128], BF16, offset=WK + 43008 + 4096)
        wq_s = A("wq_s", [128, NHT, 128], BF16, offset=ST + 8192)
        vch = [A(f"vch{i}", [128, NHT, 512], BF16, offset=WK + i * 16384) for i in range(2)]
        BIG = 98144
        big = A("big", [128, 65536], BF16, offset=BIG)
        xlnT = big[:].rearrange("p (t s) -> p t s", t=NHT)
        wdh = big[:].rearrange("p (k h) -> p k h", k=NIT)
        wosb = big[:].rearrange("p (t n) -> p t n", n=H)

        p_a = nc.alloc_psum_tensor("p_a", [128, 2048], F32)
        p_b = nc.alloc_psum_tensor("p_b", [128, 2048], F32)
        pa_bank = [p_a[:, i * 512:(i + 1) * 512] for i in range(4)]
        pb_bank = [p_b[:, i * 512:(i + 1) * 512] for i in range(4)]

        block = ent(nc.Block())
        P = Prog(nc, stack)

        b_const = P.buf("const", 20480, 20512)
        b_ones = P.buf("ones", 20512, 20768)
        b_stats = [P.buf("stats0", 20768, 20864), P.buf("stats1", 20864, 20960)]
        b_stats2 = [P.buf("stats2_0", 20960, 20992), P.buf("stats2_1", 20992, 21024)]
        b_zp = [P.buf("zp0", 21024, 21040), P.buf("zp1", 21040, 21056)]
        b_zr = [P.buf("zr0", 21056, 21072), P.buf("zr1", 21072, 21088)]
        b_gbc = P.buf("gbc", WK + 45056, WK + 45056 + IBLK * 4)
        b_ubc = P.buf("ubc", WK + 45056 + IBLK * 4, WK + 47104)
        b_stx = [P.buf(f"stx{i}", ST + i * 8192, ST + (i + 1) * 8192) for i in range(2)]
        b_stq = [P.buf(f"stq{i}", ST + i * 4096, ST + (i + 1) * 4096) for i in range(4)]
        b_tok = [P.buf("tok0", WK + 40960, WK + 45056),
                 P.buf("tok1", WK + 47104, WK + 51200)]
        b_wg = [P.buf(f"wg{i}", WK + i * 8192, WK + (i + 1) * 8192) for i in range(2)]
        b_wu = [P.buf(f"wu{i}", WK + 16384 + i * 8192, WK + 16384 + (i + 1) * 8192) for i in range(2)]
        b_sg = [P.buf(f"sg{i}", WK + 32768 + i * 1024, WK + 32768 + (i + 1) * 1024) for i in range(4)]
        b_su = [P.buf(f"su{i}", WK + 36864 + i * 1024, WK + 36864 + (i + 1) * 1024) for i in range(4)]
        b_hh = [P.buf(f"hh{i}", WK + 40960 + i * 1024, WK + 40960 + (i + 1) * 1024) for i in range(4)]
        b_hring = [P.buf(f"hring{i}", WK + i * 16384, WK + (i + 1) * 16384) for i in range(3)]
        b_wdst = [P.buf(f"wdst{i}", WK + 47104 + i * 2048, WK + 47104 + (i + 1) * 2048) for i in range(2)]
        b_wdso = [P.buf(f"wdso{i}", WK + 51200 + i * 1024, WK + 51200 + (i + 1) * 1024) for i in range(2)]
        b_dbc2 = P.buf("dbc2", 21088, 21088 + 2048)
        b_kh = [P.buf("kh0", WK, WK + 8192),
                P.buf("kh1", WK + 8192, WK + 16384)]
        b_vh = P.buf("vh", WK + 16384, WK + 24576)
        b_qring = P.buf("qring", WK + 24576, WK + 26624)
        b_eblk = [P.buf("eblk0", WK + 26624, WK + 34816),
                  P.buf("eblk1", ST, ST + 8192)]
        b_cacc = P.buf("cacc", WK + 34816, WK + 43008)
        b_cacc2 = P.buf("cacc2", ST + 12288, ST + 20480)
        b_wks = P.buf("wks", WK + 43008, WK + 47104)
        b_wvs = P.buf("wvs", WK + 47104, WK + 51200)
        b_wqs = P.buf("wqs", ST + 8192, ST + 12288)
        b_vch = [P.buf(f"vch{i}", WK + i * 16384, WK + (i + 1) * 16384) for i in range(2)]
        b_big = P.buf("big", BIG, BIG + 131072)
        b_pa = [P.buf(f"pa{i}") for i in range(4)]
        b_pb = [P.buf(f"pb{i}") for i in range(4)]
        b_xin = P.buf("x_in")
        b_xln_h = P.buf("xln_h")
        b_x1_h = [P.buf(f"x1_h{t}") for t in range(NST)]
        b_x2_h = [P.buf(f"x2_h{t}") for t in range(NST)]
        b_hact_h = P.buf("hact_h")
        b_wgbf = P.buf("wgbf")
        b_wubf = P.buf("wubf")
        b_wdbf = P.buf("wdbf")
        b_vals_h = P.buf("vals_h")
        b_out_h = P.buf("out_h")

        P.emit("vector", lambda e: e.memset(c_const[:, 0:1], EPS), writes=[b_const])
        P.emit("vector", lambda e: e.memset(c_const[:, 1:2], INV_SQRT_DH), writes=[b_const])
        P.emit("vector", lambda e: e.memset(c_ones[:], 1.0), writes=[b_ones])

        def ln_tile(t, src_h, b_src):
            par = t % 2
            sx, bx = st_x[par], b_stx[par]
            bsrc_t = b_src[t] if isinstance(b_src, list) else b_src
            bst, bst2 = b_stats[par], b_stats2[par]
            mv, rs, nmrs = c_mv[par], c_rs[par], c_nmrs[par]
            tok, btok = st_tok[par], b_tok[par]
            P.emit("scalar_dma", lambda e, t=t, sx=sx: e.dma_start(
                sx[:], src_h[t * 128:(t + 1) * 128, :]),
                reads=[bsrc_t], writes=[bx])
            for q in range(4):
                P.emit("vector", lambda e, q=q, sx=sx, par=par: e.bn_stats(
                    c_bn[:, par * 24 + q * 6:par * 24 + (q + 1) * 6],
                    sx[:, q * 512:(q + 1) * 512]),
                    reads=[bx], writes=[bst])
            P.emit("vector", lambda e, par=par, mv=mv: e.bn_aggr(
                mv, c_bn[:, par * 24:(par + 1) * 24]),
                reads=[bst], writes=[bst2])
            P.emit("scalar", lambda e, mv=mv, rs=rs: e.activation(
                rs, mv[:, 1:2], AF.Ln, bias=c_const[:, 0:1], scale=1.0),
                reads=[bst2, b_const], writes=[bst2])
            P.emit("scalar", lambda e, rs=rs: e.activation(
                rs, rs, AF.Exp, bias=0.0, scale=-0.5),
                reads=[bst2], writes=[bst2])
            P.emit("vector", lambda e, mv=mv, rs=rs, nmrs=nmrs: e.scalar_tensor_tensor(
                nmrs, mv[:, 0:1], -1.0, rs, ALU.mult, ALU.mult),
                reads=[bst2], writes=[bst2])
            P.emit("vector", lambda e, sx=sx, rs=rs, nmrs=nmrs, tok=tok: e.tensor_scalar(
                tok[:], sx[:], rs, nmrs, ALU.mult, ALU.add),
                reads=[bx, bst2], writes=[btok])
            P.emit("sync", lambda e, t=t, tok=tok: e.dma_start(
                xln_h[t * 128:(t + 1) * 128, :], tok[:]),
                reads=[btok], writes=[b_xln_h])

        def ln_transposes():
            for ht in range(NHT):
                P.emit("sync", lambda e, ht=ht: e.dma_start(
                    xlnT[:, ht, :], xln_h[:, ht * 128:(ht + 1) * 128], transpose=True),
                    reads=[b_xln_h], writes=[b_big])

        def ln_phase(src_h, b_src):
            for t in range(NST):
                ln_tile(t, src_h, b_src)
            ln_transposes()

        def gu_load_block(ib, first_call):
            par = ib % 2
            wgs, bwg = w_g[par], b_wg[par]
            wus, bwu = w_u[par], b_wu[par]
            if first_call:
                P.emit("gpsimd", lambda e, ib=ib, wgs=wgs: e.dma_start(
                    wgs[:], wgT_h[:, ib * IBLK:(ib + 1) * IBLK]
                    .rearrange("(t p) i -> p t i", p=128)), writes=[bwg])
                P.emit("gpsimd", lambda e, ib=ib, wus=wus: e.dma_start(
                    wus[:], wuT_h[:, ib * IBLK:(ib + 1) * IBLK]
                    .rearrange("(t p) i -> p t i", p=128)), writes=[bwu])
                P.emit("gpsimd", lambda e, ib=ib: e.dma_start(
                    c_gbc[:], gsc_h[:, ib * IBLK:(ib + 1) * IBLK]
                    .broadcast_to([128, IBLK])), writes=[b_gbc])
                P.emit("gpsimd", lambda e, ib=ib: e.dma_start(
                    c_ubc[:], usc_h[:, ib * IBLK:(ib + 1) * IBLK]
                    .broadcast_to([128, IBLK])), writes=[b_ubc])
            else:
                P.emit("scalar_dma", lambda e, ib=ib, wgs=wgs: e.dma_start(
                    wgs[:].rearrange("p t i -> p (t i)"), wgbf_h[ib]),
                    reads=[b_wgbf], writes=[bwg])
                P.emit("scalar_dma", lambda e, ib=ib, wus=wus: e.dma_start(
                    wus[:].rearrange("p t i -> p (t i)"), wubf_h[ib]),
                    reads=[b_wubf], writes=[bwu])

        def gu_dequant_tasks(ib):
            par = ib % 2
            wgs, bwg = w_g[par], b_wg[par]
            wus, bwu = w_u[par], b_wu[par]
            tasks = []
            for ht in range(NHT):
                tasks.append(lambda ht=ht, wgs=wgs, bwg=bwg: P.emit(
                    "vector", lambda e: e.scalar_tensor_tensor(
                        wgs[:, ht, :], wgs[:, ht, :], -128.0, c_gbc[:],
                        ALU.add, ALU.mult), reads=[b_gbc, bwg], writes=[bwg]))
                tasks.append(lambda ht=ht, wus=wus, bwu=bwu: P.emit(
                    "vector", lambda e: e.scalar_tensor_tensor(
                        wus[:, ht, :], wus[:, ht, :], -128.0, c_ubc[:],
                        ALU.add, ALU.mult), reads=[b_ubc, bwu], writes=[bwu]))
            tasks.append(lambda ib=ib, wgs=wgs, bwg=bwg: P.emit(
                "sync", lambda e: e.dma_start(
                    wgbf_h[ib], wgs[:].rearrange("p t i -> p (t i)")),
                reads=[bwg], writes=[b_wgbf]))
            tasks.append(lambda ib=ib, wus=wus, bwu=bwu: P.emit(
                "sync", lambda e: e.dma_start(
                    wubf_h[ib], wus[:].rearrange("p t i -> p (t i)")),
                reads=[bwu], writes=[b_wubf]))
            return tasks

        def gu_compute_block(ib, hook=None):
            par = ib % 2
            wgs, bwg = w_g[par], b_wg[par]
            wus, bwu = w_u[par], b_wu[par]
            for it in range(IBLK // 128):
                abs_it = ib * (IBLK // 128) + it
                for scg in range(4):
                    gpar = scg % 2
                    scs = (2 * scg, 2 * scg + 1)
                    pgs = [pa_bank[gpar * 2 + j] for j in range(2)]
                    bgs = [b_pa[gpar * 2 + j] for j in range(2)]
                    pus = [pb_bank[gpar * 2 + j] for j in range(2)]
                    bus = [b_pb[gpar * 2 + j] for j in range(2)]
                    for K in range(NHT):
                        for j in range(2):
                            P.emit("tensor", lambda e, K=K, it=it, j=j, wgs=wgs,
                                   pg=pgs[j], sc=scs[j]:
                                   e.matmul(pg, wgs[:, K, it * 128:(it + 1) * 128],
                                            xlnT[:, K, sc * 512:(sc + 1) * 512],
                                            start=(K == 0), stop=(K == NHT - 1),
                                            skip_group_check=True),
                                   reads=[bwg, b_big], writes=[bgs[j]],
                                   mm_w=("g", ib, it, K))
                        for j in range(2):
                            P.emit("tensor", lambda e, K=K, it=it, j=j, wus=wus,
                                   pu=pus[j], sc=scs[j]:
                                   e.matmul(pu, wus[:, K, it * 128:(it + 1) * 128],
                                            xlnT[:, K, sc * 512:(sc + 1) * 512],
                                            start=(K == 0), stop=(K == NHT - 1),
                                            skip_group_check=True),
                                   reads=[bwu, b_big], writes=[bus[j]],
                                   mm_w=("u", ib, it, K))
                    if hook is not None:
                        hook()
                    for j in range(2):
                        sl = gpar * 2 + j
                        sg, bsg = sg_t[sl], b_sg[sl]
                        su, bsu = su_t[sl], b_su[sl]
                        hhs, bhh = hh_t[sl], b_hh[sl]
                        P.emit("scalar", lambda e, sg=sg, pg=pgs[j]: e.activation(
                            sg[:], pg, AF.Silu), reads=[bgs[j]], writes=[bsg])
                        P.emit("scalar", lambda e, su=su, pu=pus[j]: e.copy(
                            su[:], pu), reads=[bus[j]], writes=[bsu])
                        P.emit("vector", lambda e, hhs=hhs, sg=sg, su=su: e.tensor_tensor(
                            hhs[:], sg[:], su[:], ALU.mult),
                            reads=[bsg, bsu], writes=[bhh])
                        P.emit("sync", lambda e, abs_it=abs_it, sc=scs[j], hhs=hhs:
                               e.dma_start(hact_h[abs_it, :, sc * 512:(sc + 1) * 512],
                                           hhs[:]),
                               reads=[bhh], writes=[b_hact_h])

        def wd_predeq_slices(slices):
            for s in slices:
                q, k = s // NIT, s % NIT
                if k == 0:
                    P.emit("gpsimd", lambda e, q=q: e.dma_start(
                        dbc2[:], dsc_h[:, q * 512:(q + 1) * 512]
                        .broadcast_to([128, 512])), writes=[b_dbc2])
                pr = s % 2
                P.emit("sync", lambda e, q=q, k=k, pr=pr: e.dma_start(
                    wdst[pr][:], wdT_h[k * 128:(k + 1) * 128,
                                       q * 512:(q + 1) * 512]),
                    writes=[b_wdst[pr]])
                P.emit("vector", lambda e, pr=pr: e.scalar_tensor_tensor(
                    wdso[pr][:], wdst[pr][:], -128.0, dbc2[:], ALU.add, ALU.mult),
                    reads=[b_wdst[pr], b_dbc2], writes=[b_wdso[pr]])
                P.emit("sync", lambda e, q=q, k=k, pr=pr: e.dma_start(
                    wdbf_h[q // 2, :, k * 1024 + (q % 2) * 512:
                           k * 1024 + (q % 2) * 512 + 512], wdso[pr][:]),
                    reads=[b_wdso[pr]], writes=[b_wdbf])

        def ffn_gateup(first_call, skip_prologue=False):
            tasks = []

            def drain(n):
                for _ in range(min(n, len(tasks))):
                    tasks.pop(0)()

            def hook():
                drain(6)

            if not skip_prologue:
                gu_load_block(0, first_call)
                if first_call:
                    for t in gu_dequant_tasks(0):
                        t()
                gu_load_block(1, first_call)
            for ib in range(NIB):
                if ib + 1 < NIB and first_call:
                    tasks.extend(gu_dequant_tasks(ib + 1))
                if first_call:
                    tasks.extend([lambda s=s: wd_predeq_slices([s])
                                  for s in range(ib * 8, ib * 8 + 8)])
                gu_compute_block(ib, hook=hook)
                while tasks:
                    drain(len(tasks))
                if ib + 2 < NIB:
                    gu_load_block(ib + 2, first_call)

        def ffn_down(res_src_h, b_ressrc, res_dst_h, b_resdst, post_tile=None):
            for half in range(2):
                for part in range(4):
                    P.emit("sync", lambda e, half=half, part=part: e.dma_start(
                        big[:, part * 16384:(part + 1) * 16384],
                        wdbf_h[half, :, part * 16384:(part + 1) * 16384]),
                        reads=[b_wdbf], writes=[b_big])
                for st in range(NST):
                    ring = st % 3
                    hr, bhr = hring[ring], b_hring[ring]
                    for part in range(4):
                        P.emit("gpsimd", lambda e, st=st, part=part, hr=hr: e.dma_start(
                            hr[:, part * 16:(part + 1) * 16, :],
                            hact_h[part * 16:(part + 1) * 16, :,
                                   st * 128:(st + 1) * 128]
                            .rearrange("k p s -> p k s")),
                            reads=[b_hact_h], writes=[bhr])
                    pp = (st % 2) * 2
                    pd0, bd0 = pb_bank[pp], b_pb[pp]
                    pd1, bd1 = pb_bank[pp + 1], b_pb[pp + 1]
                    for K in range(NIT):
                        P.emit("tensor", lambda e, K=K, hr=hr, pd0=pd0:
                               e.matmul(pd0, hr[:, K, :], wdh[:, K, 0:512],
                                        start=(K == 0), stop=(K == NIT - 1),
                                        skip_group_check=True),
                               reads=[bhr, b_big], writes=[bd0],
                               mm_w=("d", half, st, K))
                        P.emit("tensor", lambda e, K=K, hr=hr, pd1=pd1:
                               e.matmul(pd1, hr[:, K, :], wdh[:, K, 512:1024],
                                        start=(K == 0), stop=(K == NIT - 1),
                                        skip_group_check=True),
                               reads=[bhr, b_big], writes=[bd1],
                               mm_w=("d", half, st, K))
                    qs = st % 4
                    sq, bq = st_q[qs], b_stq[qs]
                    brs = b_ressrc[st] if isinstance(b_ressrc, list) else b_ressrc
                    P.emit("scalar_dma", lambda e, st=st, half=half, sq=sq: e.dma_start(
                        sq[:], res_src_h[st * 128:(st + 1) * 128,
                                         half * 1024:(half + 1) * 1024]),
                        reads=[brs], writes=[bq])
                    P.emit("vector", lambda e, sq=sq, pp=pp: e.scalar_tensor_tensor(
                        sq[:], p_b[:, pp * 512:(pp + 2) * 512], 0.5, sq[:],
                        ALU.mult, ALU.add),
                        reads=[bd0, bd1, bq], writes=[bq])
                    brd = b_resdst[st] if isinstance(b_resdst, list) else b_resdst
                    P.emit("sync", lambda e, st=st, half=half, sq=sq: e.dma_start(
                        res_dst_h[st * 128:(st + 1) * 128,
                                  half * 1024:(half + 1) * 1024], sq[:]),
                        reads=[bq], writes=[brd])
                    if post_tile is not None and half == 1:
                        post_tile(st)

        def attn_load_w(hd, what):
            if "q" in what:
                P.emit("gpsimd", lambda e, hd=hd: e.dma_start(
                    wq_s[:], wqkv_h[:, hd * 128:(hd + 1) * 128]
                    .rearrange("(t p) f -> p t f", p=128)), writes=[b_wqs])
            if "k" in what:
                P.emit("gpsimd", lambda e, hd=hd: e.dma_start(
                    wk_s[:], wqkv_h[:, H + hd * 128:H + (hd + 1) * 128]
                    .rearrange("(t p) f -> p t f", p=128)), writes=[b_wks])
            if "v" in what:
                P.emit("gpsimd", lambda e, hd=hd: e.dma_start(
                    wv_s[:], wqkv_h[:, 2 * H + hd * 128:2 * H + (hd + 1) * 128]
                    .rearrange("(t p) f -> p t f", p=128)), writes=[b_wvs])

        def kproj_tasks(hd):
            khs, bkhs = kh[hd % 2], b_kh[hd % 2]
            tasks = []

            def group(scp):
                for K in range(NHT):
                    for j in range(2):
                        sc = 2 * scp + j
                        P.emit("tensor", lambda e, K=K, sc=sc, j=j:
                               e.matmul(pb_bank[2 + j], wk_s[:, K, :],
                                        xlnT[:, K, sc * 512:(sc + 1) * 512],
                                        start=(K == 0), stop=(K == NHT - 1),
                                        skip_group_check=True),
                               reads=[b_wks, b_big], writes=[b_pb[2 + j]],
                               mm_w=("k", hd, scp, K))
                for j in range(2):
                    sc = 2 * scp + j
                    P.emit("scalar", lambda e, sc=sc, khs=khs, j=j: e.copy(
                        khs[:, sc * 512:(sc + 1) * 512], pb_bank[2 + j]),
                        reads=[b_pb[2 + j]], writes=[bkhs])
            for scp in range(4):
                tasks.append(lambda scp=scp: group(scp))
            return tasks

        def vproj_group(hd, scp):
            if True:
                for K in range(NHT):
                    for j in range(2):
                        sc = 2 * scp + j
                        P.emit("tensor", lambda e, K=K, sc=sc, j=j:
                               e.matmul(pb_bank[2 + j], wv_s[:, K, :],
                                        xlnT[:, K, sc * 512:(sc + 1) * 512],
                                        start=(K == 0), stop=(K == NHT - 1),
                                        skip_group_check=True),
                               reads=[b_wvs, b_big], writes=[b_pb[2 + j]],
                               mm_w=("v", hd, scp, K))
                for j in range(2):
                    sc = 2 * scp + j
                    P.emit("scalar", lambda e, sc=sc, j=j: e.copy(
                        vh[:, sc * 512:(sc + 1) * 512], pb_bank[2 + j]),
                        reads=[b_pb[2 + j]], writes=[b_vh])

        def attention():
            attn_load_w(0, "qkv")
            for t in kproj_tasks(0):
                t()
            for hd in range(NH):
                khs, bkhs = kh[hd % 2], b_kh[hd % 2]
                if hd + 1 < NH:
                    P.emit("gpsimd", lambda e, hd=hd: e.dma_start(
                        wk_s[:], wqkv_h[:, H + (hd + 1) * 128:H + (hd + 2) * 128]
                        .rearrange("(t p) f -> p t f", p=128)), writes=[b_wks])
                ktasks = kproj_tasks(hd + 1) if hd + 1 < NH else []
                for t in range(NST):
                    if t in (1, 3, 5, 7):
                        vproj_group(hd, (t - 1) // 2)
                    if t % 8 == 0:
                        for K in range(NHT):
                            for j in range(2):
                                P.emit("tensor", lambda e, K=K, t=t, j=j:
                                       e.matmul(pb_bank[2 + j], wq_s[:, K, :],
                                                xlnT[:, K, t * 128 + j * 512:
                                                     t * 128 + (j + 1) * 512],
                                                start=(K == 0), stop=(K == NHT - 1),
                                                skip_group_check=True),
                                       reads=[b_wqs, b_big], writes=[b_pb[2 + j]],
                                       mm_w=("q", hd, t, K))
                        for j in range(2):
                            P.emit("scalar", lambda e, j=j: e.copy(
                                qring[:, j * 512:(j + 1) * 512], pb_bank[2 + j]),
                                reads=[b_pb[2 + j]], writes=[b_qring])
                    if t == 24 and hd + 1 < NH:
                        P.emit("gpsimd", lambda e, hd=hd: e.dma_start(
                            wq_s[:], wqkv_h[:, (hd + 1) * 128:(hd + 2) * 128]
                            .rearrange("(t p) f -> p t f", p=128)), writes=[b_wqs])
                        P.emit("gpsimd", lambda e, hd=hd: e.dma_start(
                            wv_s[:], wqkv_h[:, 2 * H + (hd + 1) * 128:
                                            2 * H + (hd + 2) * 128]
                            .rearrange("(t p) f -> p t f", p=128)), writes=[b_wvs])
                    if t in (6, 13, 20, 27) and ktasks:
                        ktasks.pop(0)()
                    qslot = (t // 4) % 2
                    qcol = qslot * 512 + (t % 4) * 128
                    epar = t % 2
                    eb, beb = e_blk[epar], b_eblk[epar]
                    for g in range(4):
                        pg2 = p_a[:, (g % 2) * 1024:((g % 2) + 1) * 1024]
                        bg2 = [b_pa[(g % 2) * 2], b_pa[(g % 2) * 2 + 1]]
                        for hf in range(2):
                            P.emit("tensor", lambda e, g=g, hf=hf, qcol=qcol, khs=khs, pg2=pg2:
                                   e.matmul(pg2[:, hf * 512:(hf + 1) * 512],
                                            qring[:, qcol:qcol + 128],
                                            khs[:, g * 1024 + hf * 512:
                                                g * 1024 + (hf + 1) * 512],
                                            start=True, stop=True,
                                            skip_group_check=True),
                                   reads=[b_qring, bkhs], writes=[bg2[hf]],
                                   mm_w=("s", hd, t))
                        if g in (1, 3):
                            P.emit("tensor", lambda e:
                                   e.matmul(pb_bank[2][:, 0:64], c_ones[:],
                                            c_ones[:, 0:64], start=True,
                                            stop=True, skip_group_check=True),
                                   reads=[b_ones], writes=[b_pb[2]])
                        P.emit("scalar", lambda e, g=g, eb=eb, pg2=pg2, epar=epar: e.activation(
                            eb[:, g * 1024:(g + 1) * 1024], pg2, AF.Exp,
                            bias=0.0, scale=c_const[:, 1:2],
                            accum_out=c_zp[:, epar * 4 + g:epar * 4 + g + 1]),
                            reads=bg2 + [b_const], writes=[beb, b_zp[epar]])
                    P.emit("vector", lambda e, epar=epar: e.reduce_sum(
                        c_z[epar], c_zp[:, epar * 4:(epar + 1) * 4], AX.X),
                        reads=[b_zp[epar]], writes=[b_zr[epar]])
                    P.emit("vector", lambda e, epar=epar: e.reciprocal(
                        c_r[epar], c_z[epar]),
                        reads=[b_zr[epar]], writes=[b_zr[epar]])
                    for half, cc, bcc in ((0, cacc_lo, b_cacc), (1, cacc_hi, b_cacc2)):
                        if t == 0:
                            P.emit("vector", lambda e, eb=eb, epar=epar, cc=cc, half=half:
                                   e.tensor_scalar(cc[:], eb[:, half * 2048:(half + 1) * 2048],
                                                   c_r[epar], None, ALU.mult),
                                   reads=[beb, b_zr[epar]], writes=[bcc])
                        else:
                            P.emit("vector", lambda e, eb=eb, epar=epar, cc=cc, half=half:
                                   e.scalar_tensor_tensor(
                                       cc[:], eb[:, half * 2048:(half + 1) * 2048],
                                       c_r[epar], cc[:], ALU.mult, ALU.add),
                                   reads=[beb, b_zr[epar], bcc], writes=[bcc])
                cb, bcb = e_blk[0], b_eblk[0]
                vstage, bvstage = e_blk[1], b_eblk[1]
                P.emit("vector", lambda e, cb=cb: e.tensor_copy(
                    cb[:, 0:2048], cacc_lo[:]), reads=[b_cacc], writes=[bcb])
                P.emit("vector", lambda e, cb=cb: e.tensor_copy(
                    cb[:, 2048:4096], cacc_hi[:]), reads=[b_cacc2], writes=[bcb])
                for g in range(4):
                    pc = p_a[:, (g % 2) * 1024:((g % 2) + 1) * 1024]
                    bpc = [b_pa[(g % 2) * 2], b_pa[(g % 2) * 2 + 1]]
                    for hf in range(2):
                        P.emit("tensor", lambda e, g=g, hf=hf, cb=cb, pc=pc:
                               e.matmul(pc[:, hf * 512:(hf + 1) * 512], c_ones[:],
                                        cb[:, g * 1024 + hf * 512:
                                           g * 1024 + (hf + 1) * 512],
                                        start=True, stop=True, skip_group_check=True),
                               reads=[b_ones, bcb], writes=[bpc[hf]],
                               mm_w=("ones",))
                    P.emit("vector", lambda e, g=g, pc=pc, vstage=vstage: e.tensor_tensor(
                        vstage[:, g * 1024:(g + 1) * 1024],
                        vh[:, g * 1024:(g + 1) * 1024], pc, ALU.mult),
                        reads=[b_vh] + bpc, writes=[bvstage])
                P.emit("sync", lambda e, hd=hd, vstage=vstage: e.dma_start(
                    vals_h[hd], vstage[:]), reads=[bvstage], writes=[b_vals_h])

        def outproj(post_tile=None):
            for part in range(4):
                P.emit("gpsimd", lambda e, part=part: e.dma_start(
                    wosb[:, part * 4:(part + 1) * 4, :],
                    wo_h[part * 512:(part + 1) * 512, :]
                    .rearrange("(t p) n -> p t n", p=128)), writes=[b_big])
            for sc in range(8):
                vpar = sc % 2
                vc, bvc = vch[vpar], b_vch[vpar]
                for part in range(4):
                    P.emit("gpsimd", lambda e, sc=sc, part=part, vc=vc: e.dma_start(
                        vc[:, part * 4:(part + 1) * 4, :],
                        vals_h[part * 4:(part + 1) * 4, :, sc * 512:(sc + 1) * 512]
                        .rearrange("t p s -> p t s")),
                        reads=[b_vals_h], writes=[bvc])
                for st4 in range(4):
                    st = sc * 4 + st4
                    if st4 % 2 == 0:
                        pdv, bpd = p_b[:, 0:2048], [b_pb[0], b_pb[1], b_pb[2], b_pb[3]]
                    else:
                        pdv, bpd = p_a[:, 0:2048], [b_pa[0], b_pa[1], b_pa[2], b_pa[3]]
                    for K in range(NHT):
                        for q4 in range(4):
                            P.emit("tensor", lambda e, K=K, st4=st4, q4=q4, vc=vc, pdv=pdv:
                                   e.matmul(pdv[:, q4 * 512:(q4 + 1) * 512],
                                            vc[:, K, st4 * 128:(st4 + 1) * 128],
                                            wosb[:, K, q4 * 512:(q4 + 1) * 512],
                                            start=(K == 0), stop=(K == NHT - 1),
                                            skip_group_check=True),
                                   reads=[bvc, b_big], writes=[bpd[q4]],
                                   mm_w=("o", sc, st4, K))
                    xs = st % 2
                    sx, bx = st_x[xs], b_stx[xs]
                    P.emit("scalar_dma", lambda e, st=st, sx=sx: e.dma_start(
                        sx[:], x1_h[st * 128:(st + 1) * 128, :]),
                        reads=[b_x1_h[st]], writes=[bx])
                    P.emit("vector", lambda e, sx=sx, pdv=pdv: e.tensor_tensor(
                        sx[:], sx[:], pdv, ALU.add),
                        reads=bpd + [bx], writes=[bx])
                    P.emit("sync", lambda e, st=st, sx=sx: e.dma_start(
                        x2_h[st * 128:(st + 1) * 128, :], sx[:]),
                        reads=[bx], writes=[b_x2_h[st]])
                    if post_tile is not None:
                        post_tile(st)

        def ln_hook(src_h, b_src):
            def hook(st):
                if st >= 2:
                    ln_tile(st - 2, src_h, b_src)
            def tail():
                ln_tile(NST - 2, src_h, b_src)
                ln_tile(NST - 1, src_h, b_src)
                ln_transposes()
            return hook, tail

        def ph_ln1():
            gu_load_block(0, True)
            for t in gu_dequant_tasks(0):
                t()
            gu_load_block(1, True)
            ln_phase(x_h, b_xin)

        def ph_dn1():
            hook, tail = ln_hook(x1_h, b_x1_h)
            ffn_down(x_h, b_xin, x1_h, b_x1_h, post_tile=hook)
            P.cur_scope = "ln2"
            tail()

        def ph_oproj():
            hook, tail = ln_hook(x2_h, b_x2_h)
            outproj(post_tile=hook)
            P.cur_scope = "ln3"
            tail()

        phases = [
            ("ln1", ph_ln1),
            ("gu1", lambda: ffn_gateup(first_call=True, skip_prologue=True)),
            ("dn1", ph_dn1),
            ("attn", lambda: attention()),
            ("oproj", ph_oproj),
            ("gu2", lambda: ffn_gateup(first_call=False)),
            ("dn2", lambda: ffn_down(x2_h, b_x2_h, out_h, b_out_h)),
        ]
        if upto >= 9:
            upto = len(phases)
        for nm, ph in phases[:upto]:
            P.cur_scope = nm
            ph()

        counts = P.finalize(block)
        if debug:
            print("instruction counts per engine:", counts,
                  "total:", sum(counts.values()))
    return nc


def kernel(L0_attn_x, L0_attn_w_qkv, L0_attn_w_o, L0_ffn_gate_w,
           L0_ffn_gate_scale, L0_ffn_up_w, L0_ffn_up_scale, L0_ffn_down_w,
           L0_ffn_down_scale):
    nc = build()
    wgT = np.ascontiguousarray(np.asarray(L0_ffn_gate_w).T)
    wuT = np.ascontiguousarray(np.asarray(L0_ffn_up_w).T)
    wdT = np.ascontiguousarray(np.asarray(L0_ffn_down_w).T)
    wqkv = np.ascontiguousarray(np.asarray(L0_attn_w_qkv))
    wo = np.ascontiguousarray(np.asarray(L0_attn_w_o))
    gsc = np.ascontiguousarray(np.asarray(L0_ffn_gate_scale).reshape(1, FF))
    usc = np.ascontiguousarray(np.asarray(L0_ffn_up_scale).reshape(1, FF))
    dsc = np.ascontiguousarray(np.asarray(L0_ffn_down_scale).reshape(1, H))
    x = np.asarray(L0_attn_x)
    in_maps = []
    for c in range(8):
        in_maps.append({
            "x": np.ascontiguousarray(x[c]),
            "wqkv": wqkv, "wo": wo,
            "wgT": wgT, "wuT": wuT, "wdT": wdT,
            "gsc": gsc, "usc": usc, "dsc": dsc,
        })
    res = run_bass_kernel_spmd(nc, in_maps, core_ids=list(range(8)))
    out = np.stack([np.asarray(res.results[c]["out"]) for c in range(8)], axis=0)
    return out


# version 11
# speedup vs baseline: 1.1371x; 1.1371x over previous
import numpy as np
import ml_dtypes
import concourse.bass as bass
import concourse.mybir as mybir
from concourse.bass_utils import run_bass_kernel_spmd

F32 = mybir.dt.float32
BF16 = mybir.dt.bfloat16
I32 = mybir.dt.int32
AF = mybir.ActivationFunctionType
ALU = mybir.AluOpType
AX = mybir.AxisListType

S = 4096
H = 2048
FF = 8192
NH = 16
NST = S // 128
NHT = H // 128
NIT = FF // 128
IBLK = 256
NIB = FF // IBLK

INV_SQRT_DH = float(1.0 / np.sqrt(128.0))
EPS = 1e-5
ENGS = ("tensor", "vector", "scalar", "gpsimd", "sync")
SEM_ROLL = 50000


class Buf:
    __slots__ = ("name", "lo", "hi", "writers", "readers", "aliases", "dsem")

    def __init__(self, name, lo=-1, hi=-1):
        self.name = name
        self.lo, self.hi = lo, hi
        self.writers = {}
        self.readers = {}
        self.aliases = [self]
        self.dsem = None


CHANNELS = ("tensor", "vector", "scalar", "gpsimd", "sync", "scalar_dma")


class Prog:

    def __init__(self, nc, stack):
        self.nc = nc
        self.stack = stack
        self.sched = []
        self.cnt = {e: 0 for e in CHANNELS}
        self.sems = {e: [stack.enter_context(nc.semaphore(f"ch_{e}0"))] for e in CHANNELS}
        self.last_wait = {e: {} for e in CHANNELS}
        self.sbuf_bufs = []
        self.dma_sems = []
        self.cur_scope = "init"

    def buf(self, name, lo=-1, hi=-1):
        b = Buf(name, lo, hi)
        if lo >= 0:
            for o in self.sbuf_bufs:
                if o.lo < hi and lo < o.hi:
                    o.aliases.append(b)
                    b.aliases.append(o)
            self.sbuf_bufs.append(b)
        return b

    def _buf_dsem(self, b, chan):
        if b.dsem is None:
            b.dsem = {}
        if chan not in b.dsem:
            sem = self.stack.enter_context(self.nc.semaphore(f"ds_{b.name}_{chan}"))
            b.dsem[chan] = [sem, 0]
            self.dma_sems.append(b.dsem[chan])
        return b.dsem[chan]

    def emit(self, eng, fn, reads=(), writes=(), note="", mm_w=None):
        note = self.cur_scope
        chan = eng
        real_eng = "scalar" if eng == "scalar_dma" else eng
        dma = chan in ("sync", "gpsimd", "scalar_dma")
        if dma:
            tgt = None
            for b in writes:
                if b.lo >= 0:
                    tgt = b
                    break
            if tgt is None:
                for b in reads:
                    if b.lo >= 0:
                        tgt = b
                        break
            if tgt is None:
                tgt = writes[0] if writes else reads[0]
            dsem = self._buf_dsem(tgt, chan)
            sem = dsem[0]
            inc = 16
        else:
            inc = 1
            if self.cnt[chan] + inc > SEM_ROLL:
                self.sems[chan].append(self.stack.enter_context(
                    self.nc.semaphore(f"ch_{chan}{len(self.sems[chan])}")))
                self.cnt[chan] = 0
            sem = self.sems[chan][-1]
        waits = {}

        def need(ev, rel):
            if ev is None:
                return
            esem, val, echan = ev
            if echan == chan:
                if chan == "tensor":
                    return
            key = id(esem)
            if self.last_wait[chan].get(key, -1) >= val:
                return
            if key in waits:
                if val > waits[key][1]:
                    waits[key] = (esem, val)
            else:
                waits[key] = (esem, val)

        for b in reads:
            for a in b.aliases:
                for ev in a.writers.values():
                    need(ev, "raw")
        for b in writes:
            for a in b.aliases:
                for ev in a.writers.values():
                    need(ev, "waw")
                for ev in a.readers.values():
                    need(ev, "war")
        for key, (esem, val) in waits.items():
            self.last_wait[chan][key] = max(
                self.last_wait[chan].get(key, -1), val)
        if dma:
            dsem[1] += inc
            ev = (sem, dsem[1], chan)
        else:
            self.cnt[chan] += inc
            ev = (sem, self.cnt[chan], chan)
        if mm_w is not None:
            mm_w = (note,) + tuple(mm_w)
        self.sched.append((real_eng, fn, list(waits.values()), sem, inc, note, mm_w))
        for b in writes:
            old = b.writers.get(id(sem))
            if old is None or ev[1] > old[1]:
                b.writers[id(sem)] = ev
            b.readers = {}
        for b in reads:
            old = b.readers.get(id(sem))
            if old is None or ev[1] > old[1]:
                b.readers[id(sem)] = ev
        return ev

    def finalize(self, block, ldw_opt=True):
        self.ldw_opt = ldw_opt
        per_eng = {e: [] for e in ENGS}
        for eng, fn, waits, sem, inc, note, mm_w in self.sched:
            per_eng[eng].append((fn, waits, sem, inc, note, mm_w))
        totals = []
        for e in CHANNELS:
            if self.cnt[e] > 0:
                totals.append((self.sems[e][-1], self.cnt[e]))
        for sem, cnt in self.dma_sems:
            if cnt > 0:
                totals.append((sem, cnt))

        nc = self.nc

        def make(engine_name):
            def body(eng):
                cur = [None, None]
                last_w = [None]
                for fn, waits, sem, inc, note, mm_w in per_eng[engine_name]:
                    if note != cur[0]:
                        if cur[0] is not None:
                            nc.leave_named_scope(cur[0], cur[1], False)
                        sid, _ = nc.enter_named_scope(note, False)
                        cur[0], cur[1] = note, sid
                    for wsem, wval in waits:
                        eng.wait_ge(wsem, wval)
                    inst = fn(eng)
                    inst.then_inc(sem, inc)
                    if engine_name == "tensor" and self.ldw_opt:
                        if mm_w is not None and mm_w == last_w[0]:
                            self.skip_ldw_names.add(inst.ins.name)
                        last_w[0] = mm_w
                if cur[0] is not None:
                    nc.leave_named_scope(cur[0], cur[1], False)
                if engine_name == "sync":
                    for tsem, tval in totals:
                        eng.wait_ge(tsem, tval)
            return body

        self.skip_ldw_names = set()
        block.sync(make("sync"))
        block.scalar(make("scalar"))
        block.vector(make("vector"))
        block.tensor(make("tensor"))
        block.gpsimd(make("gpsimd"))
        if self.skip_ldw_names:
            for func in nc.m.functions:
                for bb in func.blocks:
                    for idx, ins in enumerate(bb.instructions):
                        if ins.name in self.skip_ldw_names:
                            ins2 = ins.__replace__(ldweights=False)
                            bb.instructions[idx] = ins2
                            nc.register_instruction(ins2, overwrite=True)
        n = {e: len(per_eng[e]) for e in ENGS}
        return n


def build(debug=False, upto=9, ldw_opt=True):
    nc = bass.Bass("TRN2", target_bir_lowering=False)

    x_h = nc.dram_tensor("x", [S, H], F32, kind="ExternalInput").ap()
    wqkv_h = nc.dram_tensor("wqkv", [H, 3 * H], F32, kind="ExternalInput").ap()
    wo_h = nc.dram_tensor("wo", [H, H], F32, kind="ExternalInput").ap()
    wgT_h = nc.dram_tensor("wgT", [H, FF], I32, kind="ExternalInput").ap()
    wuT_h = nc.dram_tensor("wuT", [H, FF], I32, kind="ExternalInput").ap()
    wdT_h = nc.dram_tensor("wdT", [FF, H], I32, kind="ExternalInput").ap()
    gsc_h = nc.dram_tensor("gsc", [1, FF], F32, kind="ExternalInput").ap()
    usc_h = nc.dram_tensor("usc", [1, FF], F32, kind="ExternalInput").ap()
    dsc_h = nc.dram_tensor("dsc", [1, H], F32, kind="ExternalInput").ap()
    out_h = nc.dram_tensor("out", [S, H], F32, kind="ExternalOutput").ap()

    dk = "ExternalOutput" if debug else "Internal"
    xln_h = nc.dram_tensor("xln_s", [S, H], BF16, kind=dk).ap()
    x1_h = nc.dram_tensor("x1_s", [S, H], F32, kind=dk).ap()
    x2_h = nc.dram_tensor("x2_s", [S, H], F32, kind=dk).ap()
    hact_h = nc.dram_tensor("hact_s", [NIT, 128, S], BF16, kind=dk).ap()
    vals_h = nc.dram_tensor("vals_s", [NH, 128, S], BF16, kind=dk).ap()
    wgbf_h = nc.dram_tensor("wgbf_s", [NIB, 128, NHT * IBLK], BF16, kind="Internal").ap()
    wubf_h = nc.dram_tensor("wubf_s", [NIB, 128, NHT * IBLK], BF16, kind="Internal").ap()
    wdbf_h = nc.dram_tensor("wdbf_s", [2, 128, NIT * 1024], BF16, kind="Internal").ap()
    dstat_h = nc.dram_tensor("dstat_s", [NST, 128, 8], F32, kind=dk).ap() if debug else None

    from contextlib import ExitStack
    with ExitStack() as stack:
        ent = stack.enter_context
        A = nc.alloc_sbuf_tensor_at

        c_const = A("c_const", [128, 8], F32, offset=20480)
        c_ones = A("c_ones", [128, 128], BF16, offset=20512)
        c_bn = A("c_bn", [128, 48], F32, offset=20768)
        c_sm = A("c_sm", [128, 16], F32, offset=20960)
        c_mv = [c_sm[:, 0:2], c_sm[:, 8:10]]
        c_rs = [c_sm[:, 2:3], c_sm[:, 10:11]]
        c_nmrs = [c_sm[:, 3:4], c_sm[:, 11:12]]
        c_zp = A("c_zp", [128, 8], F32, offset=21024)
        c_zr = A("c_zr", [128, 8], F32, offset=21056)
        c_z = [c_zr[:, 0:1], c_zr[:, 1:2]]
        c_r = [c_zr[:, 4:5], c_zr[:, 5:6]]
        ST = 23040
        st_x = [A(f"st_x{i}", [128, 2048], F32, offset=ST + i * 8192) for i in range(2)]
        st_q = [A(f"st_q{i}", [128, 1024], F32, offset=ST + i * 4096) for i in range(4)]
        st_tok = [A("st_tok0", [128, 2048], BF16, offset=43520 + 40960),
                  A("st_tok1", [128, 2048], BF16, offset=43520 + 47104)]
        WK = 43520
        w_g = [A(f"w_g{i}", [128, NHT, IBLK], BF16, offset=WK + i * 8192) for i in range(2)]
        w_u = [A(f"w_u{i}", [128, NHT, IBLK], BF16, offset=WK + 16384 + i * 8192) for i in range(2)]
        sg_t = [A(f"sg{i}", [128, 512], BF16, offset=WK + 32768 + i * 1024) for i in range(4)]
        su_t = [A(f"su{i}", [128, 512], BF16, offset=WK + 36864 + i * 1024) for i in range(4)]
        hh_t = [A(f"hh{i}", [128, 512], BF16, offset=WK + 40960 + i * 1024) for i in range(4)]
        c_gbc = A("c_gbc", [128, IBLK], F32, offset=WK + 45056)
        c_ubc = A("c_ubc", [128, IBLK], F32, offset=WK + 45056 + IBLK * 4)
        hring = [A(f"hring{i}", [128, NIT, 128], BF16, offset=WK + i * 16384) for i in range(3)]
        wdst = [A(f"wdst{i}", [128, 512], I32, offset=WK + 47104 + i * 2048) for i in range(2)]
        wdso = [A(f"wdso{i}", [128, 512], BF16, offset=WK + 51200 + i * 1024) for i in range(2)]
        dbc2 = A("dbc2", [128, 512], F32, offset=21088)
        kh = [A("kh0", [128, S], BF16, offset=WK),
              A("kh1", [128, S], BF16, offset=WK + 8192)]
        vh = A("vh", [128, S], BF16, offset=WK + 16384)
        qring = A("qring", [128, 1024], BF16, offset=WK + 24576)
        qrA = qring
        qrB = A("qrB", [128, 1024], BF16, offset=18432)
        e_blk = [A("e_blk0", [128, S], BF16, offset=WK + 26624),
                 A("e_blk1", [128, S], BF16, offset=ST)]
        cacc_lo = A("cacc_lo", [128, 2048], F32, offset=WK + 34816)
        cacc_hi = A("cacc_hi", [128, 2048], F32, offset=ST + 12288)
        wk_s = A("wk_s", [128, NHT, 128], BF16, offset=WK + 43008)
        wv_s = A("wv_s", [128, NHT, 128], BF16, offset=WK + 43008 + 4096)
        wq_s = A("wq_s", [128, NHT, 128], BF16, offset=ST + 8192)
        vch = [A(f"vch{i}", [128, NHT, 512], BF16, offset=WK + i * 16384) for i in range(2)]
        BIG = 98144
        big = A("big", [128, 65536], BF16, offset=BIG)
        xlnT = big[:].rearrange("p (t s) -> p t s", t=NHT)
        wdh = big[:].rearrange("p (k h) -> p k h", k=NIT)
        wosb = big[:].rearrange("p (t n) -> p t n", n=H)

        p_a = nc.alloc_psum_tensor("p_a", [128, 2048], F32)
        p_b = nc.alloc_psum_tensor("p_b", [128, 2048], F32)
        pa_bank = [p_a[:, i * 512:(i + 1) * 512] for i in range(4)]
        pb_bank = [p_b[:, i * 512:(i + 1) * 512] for i in range(4)]

        block = ent(nc.Block())
        P = Prog(nc, stack)

        b_const = P.buf("const", 20480, 20512)
        b_ones = P.buf("ones", 20512, 20768)
        b_stats = [P.buf("stats0", 20768, 20864), P.buf("stats1", 20864, 20960)]
        b_stats2 = [P.buf("stats2_0", 20960, 20992), P.buf("stats2_1", 20992, 21024)]
        b_zp = [P.buf("zp0", 21024, 21040), P.buf("zp1", 21040, 21056)]
        b_zr = [P.buf("zr0", 21056, 21072), P.buf("zr1", 21072, 21088)]
        b_gbc = P.buf("gbc", WK + 45056, WK + 45056 + IBLK * 4)
        b_ubc = P.buf("ubc", WK + 45056 + IBLK * 4, WK + 47104)
        b_stx = [P.buf(f"stx{i}", ST + i * 8192, ST + (i + 1) * 8192) for i in range(2)]
        b_stq = [P.buf(f"stq{i}", ST + i * 4096, ST + (i + 1) * 4096) for i in range(4)]
        b_tok = [P.buf("tok0", WK + 40960, WK + 45056),
                 P.buf("tok1", WK + 47104, WK + 51200)]
        b_wg = [P.buf(f"wg{i}", WK + i * 8192, WK + (i + 1) * 8192) for i in range(2)]
        b_wu = [P.buf(f"wu{i}", WK + 16384 + i * 8192, WK + 16384 + (i + 1) * 8192) for i in range(2)]
        b_sg = [P.buf(f"sg{i}", WK + 32768 + i * 1024, WK + 32768 + (i + 1) * 1024) for i in range(4)]
        b_su = [P.buf(f"su{i}", WK + 36864 + i * 1024, WK + 36864 + (i + 1) * 1024) for i in range(4)]
        b_hh = [P.buf(f"hh{i}", WK + 40960 + i * 1024, WK + 40960 + (i + 1) * 1024) for i in range(4)]
        b_hring = [P.buf(f"hring{i}", WK + i * 16384, WK + (i + 1) * 16384) for i in range(3)]
        b_wdst = [P.buf(f"wdst{i}", WK + 47104 + i * 2048, WK + 47104 + (i + 1) * 2048) for i in range(2)]
        b_wdso = [P.buf(f"wdso{i}", WK + 51200 + i * 1024, WK + 51200 + (i + 1) * 1024) for i in range(2)]
        b_dbc2 = P.buf("dbc2", 21088, 21088 + 2048)
        b_kh = [P.buf("kh0", WK, WK + 8192),
                P.buf("kh1", WK + 8192, WK + 16384)]
        b_vh = P.buf("vh", WK + 16384, WK + 24576)
        b_qring = P.buf("qring", WK + 24576, WK + 26624)
        b_qrA = b_qring
        b_qrB = P.buf("qrB", 18432, 20480)
        b_eblk = [P.buf("eblk0", WK + 26624, WK + 34816),
                  P.buf("eblk1", ST, ST + 8192)]
        b_cacc = P.buf("cacc", WK + 34816, WK + 43008)
        b_cacc2 = P.buf("cacc2", ST + 12288, ST + 20480)
        b_wks = P.buf("wks", WK + 43008, WK + 47104)
        b_wvs = P.buf("wvs", WK + 47104, WK + 51200)
        b_wqs = P.buf("wqs", ST + 8192, ST + 12288)
        b_vch = [P.buf(f"vch{i}", WK + i * 16384, WK + (i + 1) * 16384) for i in range(2)]
        b_big = P.buf("big", BIG, BIG + 131072)
        b_pa = [P.buf(f"pa{i}") for i in range(4)]
        b_pb = [P.buf(f"pb{i}") for i in range(4)]
        b_xin = P.buf("x_in")
        b_xln_h = P.buf("xln_h")
        b_x1_h = [P.buf(f"x1_h{t}") for t in range(NST)]
        b_x2_h = [P.buf(f"x2_h{t}") for t in range(NST)]
        b_hact_h = P.buf("hact_h")
        b_wgbf = P.buf("wgbf")
        b_wubf = P.buf("wubf")
        b_wdbf = P.buf("wdbf")
        b_vals_h = P.buf("vals_h")
        b_out_h = P.buf("out_h")

        P.emit("vector", lambda e: e.memset(c_const[:, 0:1], EPS), writes=[b_const])
        P.emit("vector", lambda e: e.memset(c_const[:, 1:2], INV_SQRT_DH), writes=[b_const])
        P.emit("vector", lambda e: e.memset(c_ones[:], 1.0), writes=[b_ones])

        def ln_tile(t, src_h, b_src):
            par = t % 2
            sx, bx = st_x[par], b_stx[par]
            bsrc_t = b_src[t] if isinstance(b_src, list) else b_src
            bst, bst2 = b_stats[par], b_stats2[par]
            mv, rs, nmrs = c_mv[par], c_rs[par], c_nmrs[par]
            tok, btok = st_tok[par], b_tok[par]
            P.emit("scalar_dma", lambda e, t=t, sx=sx: e.dma_start(
                sx[:], src_h[t * 128:(t + 1) * 128, :]),
                reads=[bsrc_t], writes=[bx])
            for q in range(4):
                P.emit("vector", lambda e, q=q, sx=sx, par=par: e.bn_stats(
                    c_bn[:, par * 24 + q * 6:par * 24 + (q + 1) * 6],
                    sx[:, q * 512:(q + 1) * 512]),
                    reads=[bx], writes=[bst])
            P.emit("vector", lambda e, par=par, mv=mv: e.bn_aggr(
                mv, c_bn[:, par * 24:(par + 1) * 24]),
                reads=[bst], writes=[bst2])
            P.emit("scalar", lambda e, mv=mv, rs=rs: e.activation(
                rs, mv[:, 1:2], AF.Ln, bias=c_const[:, 0:1], scale=1.0),
                reads=[bst2, b_const], writes=[bst2])
            P.emit("scalar", lambda e, rs=rs: e.activation(
                rs, rs, AF.Exp, bias=0.0, scale=-0.5),
                reads=[bst2], writes=[bst2])
            P.emit("vector", lambda e, mv=mv, rs=rs, nmrs=nmrs: e.scalar_tensor_tensor(
                nmrs, mv[:, 0:1], -1.0, rs, ALU.mult, ALU.mult),
                reads=[bst2], writes=[bst2])
            P.emit("vector", lambda e, sx=sx, rs=rs, nmrs=nmrs, tok=tok: e.tensor_scalar(
                tok[:], sx[:], rs, nmrs, ALU.mult, ALU.add),
                reads=[bx, bst2], writes=[btok])
            P.emit("sync", lambda e, t=t, tok=tok: e.dma_start(
                xln_h[t * 128:(t + 1) * 128, :], tok[:]),
                reads=[btok], writes=[b_xln_h])

        def ln_transposes():
            for ht in range(NHT):
                P.emit("sync", lambda e, ht=ht: e.dma_start(
                    xlnT[:, ht, :], xln_h[:, ht * 128:(ht + 1) * 128], transpose=True),
                    reads=[b_xln_h], writes=[b_big])

        def ln_phase(src_h, b_src):
            for t in range(NST):
                ln_tile(t, src_h, b_src)
            ln_transposes()

        def gu_load_block(ib, first_call):
            par = ib % 2
            wgs, bwg = w_g[par], b_wg[par]
            wus, bwu = w_u[par], b_wu[par]
            if first_call:
                P.emit("gpsimd", lambda e, ib=ib, wgs=wgs: e.dma_start(
                    wgs[:], wgT_h[:, ib * IBLK:(ib + 1) * IBLK]
                    .rearrange("(t p) i -> p t i", p=128)), writes=[bwg])
                P.emit("gpsimd", lambda e, ib=ib, wus=wus: e.dma_start(
                    wus[:], wuT_h[:, ib * IBLK:(ib + 1) * IBLK]
                    .rearrange("(t p) i -> p t i", p=128)), writes=[bwu])
                P.emit("gpsimd", lambda e, ib=ib: e.dma_start(
                    c_gbc[:], gsc_h[:, ib * IBLK:(ib + 1) * IBLK]
                    .broadcast_to([128, IBLK])), writes=[b_gbc])
                P.emit("gpsimd", lambda e, ib=ib: e.dma_start(
                    c_ubc[:], usc_h[:, ib * IBLK:(ib + 1) * IBLK]
                    .broadcast_to([128, IBLK])), writes=[b_ubc])
            else:
                P.emit("scalar_dma", lambda e, ib=ib, wgs=wgs: e.dma_start(
                    wgs[:].rearrange("p t i -> p (t i)"), wgbf_h[ib]),
                    reads=[b_wgbf], writes=[bwg])
                P.emit("scalar_dma", lambda e, ib=ib, wus=wus: e.dma_start(
                    wus[:].rearrange("p t i -> p (t i)"), wubf_h[ib]),
                    reads=[b_wubf], writes=[bwu])

        def gu_dequant_tasks(ib):
            par = ib % 2
            wgs, bwg = w_g[par], b_wg[par]
            wus, bwu = w_u[par], b_wu[par]
            tasks = []
            for ht in range(NHT):
                tasks.append(lambda ht=ht, wgs=wgs, bwg=bwg: P.emit(
                    "vector", lambda e: e.scalar_tensor_tensor(
                        wgs[:, ht, :], wgs[:, ht, :], -128.0, c_gbc[:],
                        ALU.add, ALU.mult), reads=[b_gbc, bwg], writes=[bwg]))
                tasks.append(lambda ht=ht, wus=wus, bwu=bwu: P.emit(
                    "vector", lambda e: e.scalar_tensor_tensor(
                        wus[:, ht, :], wus[:, ht, :], -128.0, c_ubc[:],
                        ALU.add, ALU.mult), reads=[b_ubc, bwu], writes=[bwu]))
            tasks.append(lambda ib=ib, wgs=wgs, bwg=bwg: P.emit(
                "sync", lambda e: e.dma_start(
                    wgbf_h[ib], wgs[:].rearrange("p t i -> p (t i)")),
                reads=[bwg], writes=[b_wgbf]))
            tasks.append(lambda ib=ib, wus=wus, bwu=bwu: P.emit(
                "sync", lambda e: e.dma_start(
                    wubf_h[ib], wus[:].rearrange("p t i -> p (t i)")),
                reads=[bwu], writes=[b_wubf]))
            return tasks

        def gu_compute_block(ib, hook=None):
            par = ib % 2
            wgs, bwg = w_g[par], b_wg[par]
            wus, bwu = w_u[par], b_wu[par]
            for it in range(IBLK // 128):
                abs_it = ib * (IBLK // 128) + it
                for scg in range(4):
                    gpar = scg % 2
                    scs = (2 * scg, 2 * scg + 1)
                    pgs = [pa_bank[gpar * 2 + j] for j in range(2)]
                    bgs = [b_pa[gpar * 2 + j] for j in range(2)]
                    pus = [pb_bank[gpar * 2 + j] for j in range(2)]
                    bus = [b_pb[gpar * 2 + j] for j in range(2)]
                    for K in range(NHT):
                        for j in range(2):
                            P.emit("tensor", lambda e, K=K, it=it, j=j, wgs=wgs,
                                   pg=pgs[j], sc=scs[j]:
                                   e.matmul(pg, wgs[:, K, it * 128:(it + 1) * 128],
                                            xlnT[:, K, sc * 512:(sc + 1) * 512],
                                            start=(K == 0), stop=(K == NHT - 1),
                                            skip_group_check=True),
                                   reads=[bwg, b_big], writes=[bgs[j]],
                                   mm_w=("g", ib, it, K))
                        for j in range(2):
                            P.emit("tensor", lambda e, K=K, it=it, j=j, wus=wus,
                                   pu=pus[j], sc=scs[j]:
                                   e.matmul(pu, wus[:, K, it * 128:(it + 1) * 128],
                                            xlnT[:, K, sc * 512:(sc + 1) * 512],
                                            start=(K == 0), stop=(K == NHT - 1),
                                            skip_group_check=True),
                                   reads=[bwu, b_big], writes=[bus[j]],
                                   mm_w=("u", ib, it, K))
                    if hook is not None:
                        hook()
                    for j in range(2):
                        sl = gpar * 2 + j
                        sg, bsg = sg_t[sl], b_sg[sl]
                        su, bsu = su_t[sl], b_su[sl]
                        hhs, bhh = hh_t[sl], b_hh[sl]
                        P.emit("scalar", lambda e, sg=sg, pg=pgs[j]: e.activation(
                            sg[:], pg, AF.Silu), reads=[bgs[j]], writes=[bsg])
                        P.emit("scalar", lambda e, su=su, pu=pus[j]: e.copy(
                            su[:], pu), reads=[bus[j]], writes=[bsu])
                        P.emit("vector", lambda e, hhs=hhs, sg=sg, su=su: e.tensor_tensor(
                            hhs[:], sg[:], su[:], ALU.mult),
                            reads=[bsg, bsu], writes=[bhh])
                        P.emit("sync", lambda e, abs_it=abs_it, sc=scs[j], hhs=hhs:
                               e.dma_start(hact_h[abs_it, :, sc * 512:(sc + 1) * 512],
                                           hhs[:]),
                               reads=[bhh], writes=[b_hact_h])

        def wd_predeq_slices(slices):
            for s in slices:
                q, k = s // NIT, s % NIT
                if k == 0:
                    P.emit("gpsimd", lambda e, q=q: e.dma_start(
                        dbc2[:], dsc_h[:, q * 512:(q + 1) * 512]
                        .broadcast_to([128, 512])), writes=[b_dbc2])
                pr = s % 2
                P.emit("sync", lambda e, q=q, k=k, pr=pr: e.dma_start(
                    wdst[pr][:], wdT_h[k * 128:(k + 1) * 128,
                                       q * 512:(q + 1) * 512]),
                    writes=[b_wdst[pr]])
                P.emit("vector", lambda e, pr=pr: e.scalar_tensor_tensor(
                    wdso[pr][:], wdst[pr][:], -128.0, dbc2[:], ALU.add, ALU.mult),
                    reads=[b_wdst[pr], b_dbc2], writes=[b_wdso[pr]])
                P.emit("sync", lambda e, q=q, k=k, pr=pr: e.dma_start(
                    wdbf_h[q // 2, :, k * 1024 + (q % 2) * 512:
                           k * 1024 + (q % 2) * 512 + 512], wdso[pr][:]),
                    reads=[b_wdso[pr]], writes=[b_wdbf])

        def ffn_gateup(first_call, skip_prologue=False):
            tasks = []

            def drain(n):
                for _ in range(min(n, len(tasks))):
                    tasks.pop(0)()

            def hook():
                drain(6)

            if not skip_prologue:
                gu_load_block(0, first_call)
                if first_call:
                    for t in gu_dequant_tasks(0):
                        t()
                gu_load_block(1, first_call)
            for ib in range(NIB):
                if ib + 1 < NIB and first_call:
                    tasks.extend(gu_dequant_tasks(ib + 1))
                if first_call:
                    tasks.extend([lambda s=s: wd_predeq_slices([s])
                                  for s in range(ib * 8, ib * 8 + 8)])
                gu_compute_block(ib, hook=hook)
                while tasks:
                    drain(len(tasks))
                if ib + 2 < NIB:
                    gu_load_block(ib + 2, first_call)

        def ffn_down(res_src_h, b_ressrc, res_dst_h, b_resdst, post_tile=None):
            for half in range(2):
                for part in range(4):
                    P.emit("sync", lambda e, half=half, part=part: e.dma_start(
                        big[:, part * 16384:(part + 1) * 16384],
                        wdbf_h[half, :, part * 16384:(part + 1) * 16384]),
                        reads=[b_wdbf], writes=[b_big])
                for st in range(NST):
                    ring = st % 3
                    hr, bhr = hring[ring], b_hring[ring]
                    for part in range(4):
                        P.emit("gpsimd", lambda e, st=st, part=part, hr=hr: e.dma_start(
                            hr[:, part * 16:(part + 1) * 16, :],
                            hact_h[part * 16:(part + 1) * 16, :,
                                   st * 128:(st + 1) * 128]
                            .rearrange("k p s -> p k s")),
                            reads=[b_hact_h], writes=[bhr])
                    pp = (st % 2) * 2
                    pd0, bd0 = pb_bank[pp], b_pb[pp]
                    pd1, bd1 = pb_bank[pp + 1], b_pb[pp + 1]
                    for K in range(NIT):
                        P.emit("tensor", lambda e, K=K, hr=hr, pd0=pd0:
                               e.matmul(pd0, hr[:, K, :], wdh[:, K, 0:512],
                                        start=(K == 0), stop=(K == NIT - 1),
                                        skip_group_check=True),
                               reads=[bhr, b_big], writes=[bd0],
                               mm_w=("d", half, st, K))
                        P.emit("tensor", lambda e, K=K, hr=hr, pd1=pd1:
                               e.matmul(pd1, hr[:, K, :], wdh[:, K, 512:1024],
                                        start=(K == 0), stop=(K == NIT - 1),
                                        skip_group_check=True),
                               reads=[bhr, b_big], writes=[bd1],
                               mm_w=("d", half, st, K))
                    qs = st % 4
                    sq, bq = st_q[qs], b_stq[qs]
                    brs = b_ressrc[st] if isinstance(b_ressrc, list) else b_ressrc
                    P.emit("scalar_dma", lambda e, st=st, half=half, sq=sq: e.dma_start(
                        sq[:], res_src_h[st * 128:(st + 1) * 128,
                                         half * 1024:(half + 1) * 1024]),
                        reads=[brs], writes=[bq])
                    P.emit("vector", lambda e, sq=sq, pp=pp: e.scalar_tensor_tensor(
                        sq[:], p_b[:, pp * 512:(pp + 2) * 512], 0.5, sq[:],
                        ALU.mult, ALU.add),
                        reads=[bd0, bd1, bq], writes=[bq])
                    brd = b_resdst[st] if isinstance(b_resdst, list) else b_resdst
                    P.emit("sync", lambda e, st=st, half=half, sq=sq: e.dma_start(
                        res_dst_h[st * 128:(st + 1) * 128,
                                  half * 1024:(half + 1) * 1024], sq[:]),
                        reads=[bq], writes=[brd])
                    if post_tile is not None and half == 1:
                        post_tile(st)

        def attn_load_w(hd, what):
            if "q" in what:
                P.emit("gpsimd", lambda e, hd=hd: e.dma_start(
                    wq_s[:], wqkv_h[:, hd * 128:(hd + 1) * 128]
                    .rearrange("(t p) f -> p t f", p=128)), writes=[b_wqs])
            if "k" in what:
                P.emit("gpsimd", lambda e, hd=hd: e.dma_start(
                    wk_s[:], wqkv_h[:, H + hd * 128:H + (hd + 1) * 128]
                    .rearrange("(t p) f -> p t f", p=128)), writes=[b_wks])
            if "v" in what:
                P.emit("gpsimd", lambda e, hd=hd: e.dma_start(
                    wv_s[:], wqkv_h[:, 2 * H + hd * 128:2 * H + (hd + 1) * 128]
                    .rearrange("(t p) f -> p t f", p=128)), writes=[b_wvs])

        def proj_group_tasks(kind, hd, scp=None, tblk=None):
            if kind == "k":
                ws, bws = wk_s, b_wks
                dst, bdst = kh[hd % 2], b_kh[hd % 2]
            elif kind == "v":
                ws, bws = wv_s, b_wvs
                dst, bdst = vh, b_vh
            else:
                ws, bws = wq_s, b_wqs
                if (tblk // 8) % 2 == 0:
                    dst, bdst = qrA, b_qrA
                else:
                    dst, bdst = qrB, b_qrB
            tasks = []
            for K in range(NHT):
                def tk(K=K):
                    for j in range(2):
                        if kind == "q":
                            c0 = tblk * 128 + j * 512
                        else:
                            c0 = (2 * scp + j) * 512
                        P.emit("tensor", lambda e, K=K, j=j, c0=c0:
                               e.matmul(pb_bank[2 + j], ws[:, K, :],
                                        xlnT[:, K, c0:c0 + 512],
                                        start=(K == 0), stop=(K == NHT - 1),
                                        skip_group_check=True),
                               reads=[bws, b_big], writes=[b_pb[2 + j]],
                               mm_w=(kind, hd, scp if scp is not None else tblk, K))
                    if K == NHT - 1:
                        for j in range(2):
                            if kind == "q":
                                d0 = j * 512
                            else:
                                d0 = (2 * scp + j) * 512
                            P.emit("scalar", lambda e, j=j, d0=d0: e.copy(
                                dst[:, d0:d0 + 512], pb_bank[2 + j]),
                                reads=[b_pb[2 + j]], writes=[bdst])
                tasks.append(("mm", tk))
            return tasks

        def attention():
            attn_load_w(0, "qkv")
            for scp in range(4):
                for _, tk in proj_group_tasks("k", 0, scp=scp):
                    tk()
            for _, tk in proj_group_tasks("q", 0, tblk=0):
                tk()
            attn_load_w(1, "k")
            for hd in range(NH):
                khs, bkhs = kh[hd % 2], b_kh[hd % 2]
                tasks = []
                tasks += [("gate", 0)] + proj_group_tasks("q", hd, tblk=8)
                if hd + 1 < NH:
                    tasks += proj_group_tasks("k", hd + 1, scp=0)
                    tasks += proj_group_tasks("k", hd + 1, scp=1)
                tasks += [("gate", 8)] + proj_group_tasks("q", hd, tblk=16)
                tasks += proj_group_tasks("v", hd, scp=0)
                if hd + 1 < NH:
                    tasks += proj_group_tasks("k", hd + 1, scp=2)
                tasks += [("gate", 16)] + proj_group_tasks("q", hd, tblk=24)
                if hd + 1 < NH:
                    tasks.append(("free", lambda hd=hd: attn_load_w(hd + 1, "q")))
                tasks += proj_group_tasks("v", hd, scp=1)
                if hd + 1 < NH:
                    tasks += proj_group_tasks("k", hd + 1, scp=3)
                if hd + 2 < NH:
                    tasks.append(("free", lambda hd=hd: attn_load_w(hd + 2, "k")))
                if hd + 1 < NH:
                    tasks += [("gate", 24)] + proj_group_tasks("q", hd + 1, tblk=0)
                tasks += proj_group_tasks("v", hd, scp=2)
                tasks += proj_group_tasks("v", hd, scp=3)
                if hd + 1 < NH:
                    tasks.append(("free", lambda hd=hd: attn_load_w(hd + 1, "v")))

                def drain(n, cur_t):
                    done = 0
                    while tasks and done < n:
                        kind0, payload = tasks[0]
                        if kind0 == "gate":
                            if cur_t >= payload:
                                tasks.pop(0)
                                continue
                            break
                        tasks.pop(0)
                        payload()
                        if kind0 == "mm":
                            done += 1

                for t in range(NST):
                    qsrc, bq = (qrA, b_qrA) if (t // 8) % 2 == 0 else (qrB, b_qrB)
                    qcol = (t % 8) * 128
                    epar = t % 2
                    eb, beb = e_blk[epar], b_eblk[epar]
                    for g in range(4):
                        pg2 = p_a[:, (g % 2) * 1024:((g % 2) + 1) * 1024]
                        bg2 = [b_pa[(g % 2) * 2], b_pa[(g % 2) * 2 + 1]]
                        for hf in range(2):
                            P.emit("tensor", lambda e, g=g, hf=hf, qcol=qcol,
                                   qsrc=qsrc, khs=khs, pg2=pg2:
                                   e.matmul(pg2[:, hf * 512:(hf + 1) * 512],
                                            qsrc[:, qcol:qcol + 128],
                                            khs[:, g * 1024 + hf * 512:
                                                g * 1024 + (hf + 1) * 512],
                                            start=True, stop=True,
                                            skip_group_check=True),
                                   reads=[bq, bkhs], writes=[bg2[hf]],
                                   mm_w=("s", hd, t))
                        P.emit("scalar", lambda e, g=g, eb=eb, pg2=pg2, epar=epar: e.activation(
                            eb[:, g * 1024:(g + 1) * 1024], pg2, AF.Exp,
                            bias=0.0, scale=c_const[:, 1:2],
                            accum_out=c_zp[:, epar * 4 + g:epar * 4 + g + 1]),
                            reads=bg2 + [b_const], writes=[beb, b_zp[epar]])
                        drain(1, t)
                    P.emit("vector", lambda e, epar=epar: e.reduce_sum(
                        c_z[epar], c_zp[:, epar * 4:(epar + 1) * 4], AX.X),
                        reads=[b_zp[epar]], writes=[b_zr[epar]])
                    P.emit("vector", lambda e, epar=epar: e.reciprocal(
                        c_r[epar], c_z[epar]),
                        reads=[b_zr[epar]], writes=[b_zr[epar]])
                    for half, cc, bcc in ((0, cacc_lo, b_cacc), (1, cacc_hi, b_cacc2)):
                        if t == 0:
                            P.emit("vector", lambda e, eb=eb, epar=epar, cc=cc, half=half:
                                   e.tensor_scalar(cc[:], eb[:, half * 2048:(half + 1) * 2048],
                                                   c_r[epar], None, ALU.mult),
                                   reads=[beb, b_zr[epar]], writes=[bcc])
                        else:
                            P.emit("vector", lambda e, eb=eb, epar=epar, cc=cc, half=half:
                                   e.scalar_tensor_tensor(
                                       cc[:], eb[:, half * 2048:(half + 1) * 2048],
                                       c_r[epar], cc[:], ALU.mult, ALU.add),
                                   reads=[beb, b_zr[epar], bcc], writes=[bcc])
                    drain(2, t)
                drain(10000, NST)
                cb, bcb = e_blk[0], b_eblk[0]
                P.emit("vector", lambda e, cb=cb: e.tensor_copy(
                    cb[:, 0:2048], cacc_lo[:]), reads=[b_cacc], writes=[bcb])
                P.emit("vector", lambda e, cb=cb: e.tensor_copy(
                    cb[:, 2048:4096], cacc_hi[:]), reads=[b_cacc2], writes=[bcb])
                for g in range(4):
                    pc = p_a[:, (g % 2) * 1024:((g % 2) + 1) * 1024]
                    bpc = [b_pa[(g % 2) * 2], b_pa[(g % 2) * 2 + 1]]
                    for hf in range(2):
                        P.emit("tensor", lambda e, g=g, hf=hf, cb=cb, pc=pc:
                               e.matmul(pc[:, hf * 512:(hf + 1) * 512], c_ones[:],
                                        cb[:, g * 1024 + hf * 512:
                                           g * 1024 + (hf + 1) * 512],
                                        start=True, stop=True, skip_group_check=True),
                               reads=[b_ones, bcb], writes=[bpc[hf]],
                               mm_w=("ones",))
                    vstage, bvstage = e_blk[1], b_eblk[1]
                    P.emit("vector", lambda e, g=g, pc=pc, vstage=vstage: e.tensor_tensor(
                        vstage[:, g * 1024:(g + 1) * 1024],
                        vh[:, g * 1024:(g + 1) * 1024], pc, ALU.mult),
                        reads=[b_vh] + bpc, writes=[bvstage])
                    if g % 2 == 1:
                        P.emit("sync", lambda e, hd=hd, g=g, vstage=vstage: e.dma_start(
                            vals_h[hd, :, (g // 2) * 2048:(g // 2 + 1) * 2048],
                            vstage[:, (g // 2) * 2048:(g // 2 + 1) * 2048]),
                            reads=[bvstage], writes=[b_vals_h])

        def outproj(post_tile=None):
            for part in range(4):
                P.emit("gpsimd", lambda e, part=part: e.dma_start(
                    wosb[:, part * 4:(part + 1) * 4, :],
                    wo_h[part * 512:(part + 1) * 512, :]
                    .rearrange("(t p) n -> p t n", p=128)), writes=[b_big])
            for sc in range(8):
                vpar = sc % 2
                vc, bvc = vch[vpar], b_vch[vpar]
                for part in range(4):
                    P.emit("gpsimd", lambda e, sc=sc, part=part, vc=vc: e.dma_start(
                        vc[:, part * 4:(part + 1) * 4, :],
                        vals_h[part * 4:(part + 1) * 4, :, sc * 512:(sc + 1) * 512]
                        .rearrange("t p s -> p t s")),
                        reads=[b_vals_h], writes=[bvc])
                for st4 in range(4):
                    st = sc * 4 + st4
                    if st4 % 2 == 0:
                        pdv, bpd = p_b[:, 0:2048], [b_pb[0], b_pb[1], b_pb[2], b_pb[3]]
                    else:
                        pdv, bpd = p_a[:, 0:2048], [b_pa[0], b_pa[1], b_pa[2], b_pa[3]]
                    for K in range(NHT):
                        for q4 in range(4):
                            P.emit("tensor", lambda e, K=K, st4=st4, q4=q4, vc=vc, pdv=pdv:
                                   e.matmul(pdv[:, q4 * 512:(q4 + 1) * 512],
                                            vc[:, K, st4 * 128:(st4 + 1) * 128],
                                            wosb[:, K, q4 * 512:(q4 + 1) * 512],
                                            start=(K == 0), stop=(K == NHT - 1),
                                            skip_group_check=True),
                                   reads=[bvc, b_big], writes=[bpd[q4]],
                                   mm_w=("o", sc, st4, K))
                    xs = st % 2
                    sx, bx = st_x[xs], b_stx[xs]
                    P.emit("scalar_dma", lambda e, st=st, sx=sx: e.dma_start(
                        sx[:], x1_h[st * 128:(st + 1) * 128, :]),
                        reads=[b_x1_h[st]], writes=[bx])
                    P.emit("vector", lambda e, sx=sx, pdv=pdv: e.tensor_tensor(
                        sx[:], sx[:], pdv, ALU.add),
                        reads=bpd + [bx], writes=[bx])
                    P.emit("sync", lambda e, st=st, sx=sx: e.dma_start(
                        x2_h[st * 128:(st + 1) * 128, :], sx[:]),
                        reads=[bx], writes=[b_x2_h[st]])
                    if post_tile is not None:
                        post_tile(st)

        def ln_hook(src_h, b_src):
            def hook(st):
                if st >= 2:
                    ln_tile(st - 2, src_h, b_src)
            def tail():
                ln_tile(NST - 2, src_h, b_src)
                ln_tile(NST - 1, src_h, b_src)
                ln_transposes()
            return hook, tail

        def ph_ln1():
            gu_load_block(0, True)
            for t in gu_dequant_tasks(0):
                t()
            gu_load_block(1, True)
            ln_phase(x_h, b_xin)

        def ph_dn1():
            hook, tail = ln_hook(x1_h, b_x1_h)
            ffn_down(x_h, b_xin, x1_h, b_x1_h, post_tile=hook)
            P.cur_scope = "ln2"
            tail()

        def ph_oproj():
            hook, tail = ln_hook(x2_h, b_x2_h)
            outproj(post_tile=hook)
            P.cur_scope = "ln3"
            tail()

        phases = [
            ("ln1", ph_ln1),
            ("gu1", lambda: ffn_gateup(first_call=True, skip_prologue=True)),
            ("dn1", ph_dn1),
            ("attn", lambda: attention()),
            ("oproj", ph_oproj),
            ("gu2", lambda: ffn_gateup(first_call=False)),
            ("dn2", lambda: ffn_down(x2_h, b_x2_h, out_h, b_out_h)),
        ]
        if upto >= 9:
            upto = len(phases)
        for nm, ph in phases[:upto]:
            P.cur_scope = nm
            ph()

        counts = P.finalize(block, ldw_opt=ldw_opt)
        if debug:
            print("instruction counts per engine:", counts,
                  "total:", sum(counts.values()))
    return nc


def kernel(L0_attn_x, L0_attn_w_qkv, L0_attn_w_o, L0_ffn_gate_w,
           L0_ffn_gate_scale, L0_ffn_up_w, L0_ffn_up_scale, L0_ffn_down_w,
           L0_ffn_down_scale):
    nc = build()
    wgT = np.ascontiguousarray(np.asarray(L0_ffn_gate_w).T)
    wuT = np.ascontiguousarray(np.asarray(L0_ffn_up_w).T)
    wdT = np.ascontiguousarray(np.asarray(L0_ffn_down_w).T)
    wqkv = np.ascontiguousarray(np.asarray(L0_attn_w_qkv))
    wo = np.ascontiguousarray(np.asarray(L0_attn_w_o))
    gsc = np.ascontiguousarray(np.asarray(L0_ffn_gate_scale).reshape(1, FF))
    usc = np.ascontiguousarray(np.asarray(L0_ffn_up_scale).reshape(1, FF))
    dsc = np.ascontiguousarray(np.asarray(L0_ffn_down_scale).reshape(1, H))
    x = np.asarray(L0_attn_x)
    in_maps = []
    for c in range(8):
        in_maps.append({
            "x": np.ascontiguousarray(x[c]),
            "wqkv": wqkv, "wo": wo,
            "wgT": wgT, "wuT": wuT, "wdT": wdT,
            "gsc": gsc, "usc": usc, "dsc": dsc,
        })
    res = run_bass_kernel_spmd(nc, in_maps, core_ids=list(range(8)))
    out = np.stack([np.asarray(res.results[c]["out"]) for c in range(8)], axis=0)
    return out
